# version 41
# baseline (speedup 1.0000x reference)
import numpy as np
import ml_dtypes
import concourse.bass as bass
import concourse.mybir as mybir
from concourse.bass_utils import run_bass_kernel_spmd

F32 = mybir.dt.float32
BF16 = mybir.dt.bfloat16
AF = mybir.ActivationFunctionType
ALU = mybir.AluOpType
AX = mybir.AxisListType

ENGINES = ("pe", "act", "dve", "pool", "sync")
BLOCK_ATTR = {"pe": "tensor", "act": "scalar", "dve": "vector",
              "pool": "gpsimd", "sync": "sync"}
NCHAN = 8


class _Op:
    __slots__ = ("engine", "fn", "reads", "writes", "pos", "is_dma",
                 "need_inc", "waits", "tag", "chan", "seq")

    def __init__(self, engine, fn, reads, writes, is_dma, tag):
        self.engine = engine
        self.fn = fn
        self.reads = reads
        self.writes = writes
        self.is_dma = is_dma
        self.pos = -1
        self.need_inc = False
        self.waits = []
        self.tag = tag
        self.chan = None
        self.seq = 0


class Prog:
    def __init__(self):
        self.ops = {e: [] for e in ENGINES}
        self.regions = {}
        self._dma_rr = {}
        self._chan_last = {}

    def op(self, engine, fn, reads=(), writes=(), tag=""):
        return self._add(_Op(engine, fn, tuple(reads), tuple(writes), False, tag))

    def dma(self, queue, fn, reads=(), writes=(), tag="", chan=None):
        o = _Op(queue, fn, tuple(reads), tuple(writes), True, tag)
        if chan is None:
            chan = self._dma_rr.get(queue, 0) % NCHAN
            self._dma_rr[queue] = self._dma_rr.get(queue, 0) + 1
        key = (queue, chan)
        prev = self._chan_last.get(key)
        o.chan = key
        o.seq = (prev.seq + 1) if prev is not None else 1
        self._chan_last[key] = o
        if prev is not None:
            o.waits.append(prev)
        return self._add(o)

    def _add(self, o):
        o.pos = len(self.ops[o.engine])
        self.ops[o.engine].append(o)
        for r in o.reads:
            w, readers = self.regions.get(r, (None, []))
            if w is not None:
                self._edge(w, o)
            readers.append(o)
            self.regions[r] = (w, readers)
        for r in o.writes:
            w, readers = self.regions.get(r, (None, []))
            if w is not None:
                self._edge(w, o)
            for rd in readers:
                if rd is not o:
                    self._edge(rd, o)
            self.regions[r] = (o, [])
        return o

    def _edge(self, producer, consumer):
        if producer is consumer:
            return
        if (producer.engine == consumer.engine and producer.engine == "pe"
                and not producer.is_dma and not consumer.is_dma):
            return
        producer.need_inc = True
        consumer.waits.append(producer)

    def emit(self, nc):
        sem_val_after = {}
        counters = {}
        for e in ENGINES:
            for o in self.ops[e]:
                if o.is_dma:
                    key = f"dma_{o.chan[0]}_{o.chan[1]}"
                    counters[key] = counters.get(key, 0) + 16
                    sem_val_after[o] = (key, 16 * o.seq)
                elif o.need_inc:
                    key = "eng_" + e
                    counters[key] = counters.get(key, 0) + 1
                    sem_val_after[o] = (key, counters[key])
        import contextlib
        es = contextlib.ExitStack()
        sems = {k: es.enter_context(nc.semaphore(k)) for k in counters}
        with es, nc.Block() as block:
            for e in ENGINES:
                if not self.ops[e]:
                    continue

                def make_body(eng_name):
                    def body(eng):
                        highwater = {}
                        for o in self.ops[eng_name]:
                            needed = {}
                            for prod in o.waits:
                                if prod not in sem_val_after:
                                    raise RuntimeError(
                                        f"dep of {o.tag} on {prod.tag} lost")
                                k, v = sem_val_after[prod]
                                if needed.get(k, 0) < v:
                                    needed[k] = v
                            for k, v in needed.items():
                                if highwater.get(k, 0) < v:
                                    eng.wait_ge(sems[k], v)
                                    highwater[k] = v
                            inst = o.fn(eng)
                            if o in sem_val_after:
                                k, _ = sem_val_after[o]
                                inst.then_inc(sems[k], 16 if o.is_dma else 1)
                        for k, v in counters.items():
                            if highwater.get(k, 0) < v:
                                eng.wait_ge(sems[k], v)
                    return body

                getattr(block, BLOCK_ATTR[e])(make_body(e))


NB = 2
T = 2048
D = 1024
NTC = 16
NDC = 8
NFC = 8
NQW = 4
KCN = 16
EPS = 1e-5
SCALE = 0.125
INVD = 1.0 / D


def build_nc():
    nc = bass.Bass("TRN2", target_bir_lowering=False)
    x_d = nc.dram_tensor("x", [NB, T, D], F32, kind="ExternalInput").ap()
    w_d = {k: nc.dram_tensor(k, [D, D], F32, kind="ExternalInput").ap()
           for k in ("wk", "wq", "wv", "wu", "wf")}
    bu_d = nc.dram_tensor("bu", [1, D], F32, kind="ExternalInput").ap()
    disha_d = nc.dram_tensor("disha", [64, D], F32, kind="ExternalInput").ap()
    mku_d = nc.dram_tensor("mku", [D, D], BF16, kind="ExternalInput").ap()
    mkl_d = nc.dram_tensor("mkl", [D, D], BF16, kind="ExternalInput").ap()
    idn_d = nc.dram_tensor("idn", [128, 128], BF16, kind="ExternalInput").ap()
    out_d = nc.dram_tensor("out", [NB, T, D], F32, kind="ExternalOutput").ap()
    xp_d = nc.dram_tensor("xp", [T, D], F32, kind="Internal").ap()
    rrd_t = nc.dram_tensor("rrd", [2, 1024], F32, kind="Internal")
    rrd = rrd_t.ap()

    import contextlib
    es = contextlib.ExitStack()
    sb = lambda n, sh, dt: es.enter_context(nc.sbuf_tensor(n, sh, dt))
    pst = lambda n: es.enter_context(nc.psum_tensor(n, [128, 1024], F32))

    lnxt = sb("lnxt", [128, NDC * T], BF16)
    aofm = sb("aofm", [128, NFC * T], BF16)
    slotA = sb("slotA", [128, NDC * D], BF16)
    slotB = sb("slotB", [128, NDC * D], BF16)
    slotC = sb("slotC", [128, NDC * D], BF16)
    qf = [sb("qf0", [128, T], BF16), sb("qf1", [128, T], BF16)]
    kf = [sb("kf0", [128, T], BF16), sb("kf1", [128, T], BF16)]
    vp = [sb("vp0", [128, NTC * 130], BF16), sb("vp1", [128, NTC * 130], BF16)]
    pt = [sb("pt0", [128, 1024], BF16), sb("pt1", [128, 1024], BF16)]
    xt = [sb(f"xt{i}", [128, D], F32) for i in range(3)]
    yt = [sb(f"yt{i}", [128, D], F32) for i in range(2)]
    jnk = sb("jnk", [128, D], F32)
    lnb = [sb("lnb0", [128, D], BF16), sb("lnb1", [128, D], BF16)]
    rbsb = sb("rbsb", [64, 1024], F32)
    rstash = sb("rstash", [128, NTC], F32)
    nstash = sb("nstash", [128, NTC], F32)
    sxal = sb("sxal", [128, NTC], F32)
    sqal = sb("sqal", [128, NTC], F32)
    mini = sb("mini", [128, 2 * D], BF16)
    bub = sb("bub", [1, D], BF16)
    onesbr = sb("onesbr", [1, 128], BF16)
    stg2 = sb("stg2", [64, 1024], BF16)
    fbuf = [sb("fbuf0", [128, D], F32), sb("fbuf1", [128, D], F32)]
    d2 = sb("d2", [128, D], BF16)
    onescf = sb("onescf", [1, 128], F32)
    ones128 = sb("ones128", [128, 1], BF16)
    idn = sb("idn_s", [128, 128], BF16)
    wcp = [sb("wc2a", [128, D], BF16), sb("wc2b", [128, D], BF16)]
    wc2 = wcp[0]
    mkt = [sb("mkt0", [128, 128], BF16), sb("mkt1", [128, 128], BF16)]
    dstg = jnk

    def make_fam(pfx):
        names = ("sx", "sq", "t1", "q1", "t1s", "var", "lnv", "rstd", "nbv")
        return {"pfx": pfx, **{n: [sb(f"{pfx}{n}{j}", [128, 1], F32)
                                   for j in range(2)] for n in names}}
    fam1 = make_fam("fa")
    fam2 = make_fam("fb")
    fam3 = make_fam("fc")

    def fk(fam, n, j):
        return f"{fam['pfx']}{n}{j}"

    psS = [pst("psS0"), pst("psS1")]
    psO = pst("psO")
    psM = pst("psM")
    PSS = ["psS0", "psS1"]

    p = Prog()

    p.dma("sync", lambda e: e.dma_start(idn[:], idn_d), writes=["idn"], tag="ld_idn")
    p.op("dve", lambda e: e.memset(onescf[:], 1.0), writes=["onescf"], tag="ones_f")
    p.op("dve", lambda e: e.memset(ones128[:], 1.0), writes=["ones128"], tag="ones_b")
    p.dma("sync", lambda e: e.dma_start(dstg[0:1, :], bu_d), writes=["jnk"], tag="ld_bu")
    p.op("act", lambda e: e.copy(bub[:], dstg[0:1, :]), reads=["jnk"],
         writes=["bub"], tag="bu_b16")
    p.op("dve", lambda e: e.memset(onesbr[:], 1.0), writes=["onesbr"], tag="ones_br")
    p.dma("sync", lambda e: e.dma_start(dstg[0:64, :], disha_d), writes=["jnk"], tag="ld_d0")
    p.dma("sync", lambda e: e.dma_start(dstg[64:128, :], disha_d), writes=["jnk"], tag="ld_d1")
    p.op("act", lambda e: e.copy(d2[:], dstg[:]), reads=["jnk"], writes=["d2"], tag="d2c")

    def prep_weight_thunks(wd, slot_t, slot_key, mask_d, tagp, tri=None,
                           widx=0, evac_eng="dve"):
        ths = []
        for fb in range(NDC):
            def th(fb=fb):
                if tri == "low":
                    lo, hi = 0, fb + 1
                elif tri == "up":
                    lo, hi = fb, NDC
                else:
                    lo, hi = 0, NDC
                wi = (fb + widx) % 2
                wc = wcp[wi]
                wck = f"wc2{wi}"
                p.dma("pool", lambda e: e.dma_start(
                    wc[:, lo * 128:hi * 128],
                    wd[fb * 128:(fb + 1) * 128, lo * 128:hi * 128]),
                    writes=[wck], tag=f"{tagp}_ld{fb}")
                if mask_d is not None:
                    p.dma("sync", lambda e: e.dma_start(
                        mkt[wi][:, 0:128],
                        mask_d[fb * 128:(fb + 1) * 128, fb * 128:(fb + 1) * 128]),
                        writes=[f"mkt{wi}"], tag=f"{tagp}_ldm{fb}")
                    p.op("dve", lambda e: e.tensor_mul(
                        wc[:, fb * 128:(fb + 1) * 128],
                        wc[:, fb * 128:(fb + 1) * 128], mkt[wi][:, 0:128]),
                        reads=[f"mkt{wi}", wck], writes=[wck],
                        tag=f"{tagp}_m{fb}")
                for dc in range(lo, hi):
                    p.op("pe", lambda e, dc=dc: e.matmul(
                        psO[:, dc * 128:(dc + 1) * 128],
                        wc[:, dc * 128:(dc + 1) * 128],
                        idn[:], start=True, stop=True),
                        reads=[wck, "idn"], writes=["psO"],
                        tag=f"{tagp}_t{fb}_{dc}")
                sdst = lambda: slot_t[:].rearrange(
                    "p (dc f) -> p dc f", dc=NDC)[:, lo:hi, fb * 128:(fb + 1) * 128]
                ssrc = lambda: psO[:, :].rearrange(
                    "p (dc f) -> p dc f", dc=NDC)[:, lo:hi, :]
                if evac_eng == "act":
                    p.op("act", lambda e: e.activation(
                        sdst(), ssrc(), AF.Identity),
                        reads=["psO"], writes=[(slot_key, fb)],
                        tag=f"{tagp}_e{fb}")
                else:
                    p.op("dve", lambda e: e.tensor_copy(sdst(), ssrc()),
                         reads=["psO"], writes=[(slot_key, fb)],
                         tag=f"{tagp}_e{fb}")
            ths.append(th)
        return ths

    def mini_prep_thunk(wd, mask_d, base, mkey, tagp, ndc=NDC):
        def th():
            p.dma("pool", lambda e: e.dma_start(
                wc2[:, 0:ndc * 128], wd[0:128, 0:ndc * 128]),
                writes=["wc20"], tag=f"{tagp}_ld")
            if mask_d is not None:
                p.dma("sync", lambda e: e.dma_start(
                    mkt[0][:, 0:128], mask_d[0:128, 0:128]), writes=["mkt0"],
                    tag=f"{tagp}_lm")
                p.op("dve", lambda e: e.tensor_mul(
                    wc2[:, 0:128], wc2[:, 0:128], mkt[0][:, 0:128]),
                    reads=["mkt0", "wc20"], writes=["wc20"], tag=f"{tagp}_m")
            for dc in range(ndc):
                p.op("pe", lambda e, dc=dc: e.matmul(
                    psO[:, dc * 128:(dc + 1) * 128],
                    wc2[:, dc * 128:(dc + 1) * 128],
                    idn[:], start=True, stop=True),
                    reads=["wc20", "idn"], writes=["psO"], tag=f"{tagp}_t{dc}")
            p.op("dve", lambda e: e.tensor_copy(
                mini[:, base:base + ndc * 128], psO[:, 0:ndc * 128]),
                reads=["psO"], writes=[mkey], tag=f"{tagp}_e")
        return th

    def weff_add_disha(tagp):
        for dc in range(NDC):
            p.op("dve", lambda e, dc=dc: e.tensor_add(
                slotB[:, dc * D:dc * D + D], slotB[:, dc * D:dc * D + D], d2[:]),
                reads=["slotB", "d2"], writes=["slotB"], tag=f"{tagp}_{dc}")

    def stats_tail(fam, j, tagp, with_nb=True, eng="dve"):
        p.op(eng, lambda e: e.tensor_scalar(
            fam["t1"][j][:], fam["sx"][j][:], INVD, None, ALU.mult),
            reads=[fk(fam, "sx", j)], writes=[fk(fam, "t1", j)], tag=f"{tagp}_t1")
        p.op(eng, lambda e: e.tensor_scalar(
            fam["q1"][j][:], fam["sq"][j][:], INVD, EPS, ALU.mult, ALU.add),
            reads=[fk(fam, "sq", j)], writes=[fk(fam, "q1", j)], tag=f"{tagp}_q1")
        p.op(eng, lambda e: e.tensor_scalar(
            fam["t1s"][j][:], fam["t1"][j][:], fam["t1"][j][:], None, ALU.mult),
            reads=[fk(fam, "t1", j)], writes=[fk(fam, "t1s", j)], tag=f"{tagp}_t1s")
        p.op(eng, lambda e: e.tensor_sub(
            fam["var"][j][:], fam["q1"][j][:], fam["t1s"][j][:]),
            reads=[fk(fam, "q1", j), fk(fam, "t1s", j)],
            writes=[fk(fam, "var", j)], tag=f"{tagp}_var")
        p.op("act", lambda e: e.activation(
            fam["lnv"][j][:], fam["var"][j][:], AF.Ln),
            reads=[fk(fam, "var", j)], writes=[fk(fam, "lnv", j)], tag=f"{tagp}_lnv")
        p.op("act", lambda e: e.activation(
            fam["rstd"][j][:], fam["lnv"][j][:], AF.Exp, scale=-0.5),
            reads=[fk(fam, "lnv", j)], writes=[fk(fam, "rstd", j)], tag=f"{tagp}_rstd")
        if with_nb:
            p.op(eng, lambda e: e.tensor_scalar(
                fam["nbv"][j][:], fam["t1"][j][:], fam["rstd"][j][:], -1.0,
                ALU.mult, ALU.mult),
                reads=[fk(fam, "t1", j), fk(fam, "rstd", j)],
                writes=[fk(fam, "nbv", j)], tag=f"{tagp}_nb")

    def ln_stats(src_t, src_key, fam, j, tagp):
        p.op("dve", lambda e: e.tensor_reduce(
            fam["sx"][j][:], src_t[:], AX.X, ALU.add),
            reads=[src_key], writes=[fk(fam, "sx", j)], tag=f"{tagp}_sx")
        p.op("dve", lambda e: e.scalar_tensor_tensor(
            jnk[:], src_t[:], 1.0, src_t[:], ALU.mult, ALU.mult,
            accum_out=fam["sq"][j][:]),
            reads=[src_key], writes=["jnk", fk(fam, "sq", j)], tag=f"{tagp}_sq")
        stats_tail(fam, j, tagp)

    def ln_apply(dst_t, dst_key, src_t, src_key, fam, j, tagp):
        p.op("act", lambda e: e.activation(
            dst_t[:], src_t[:], AF.Identity, bias=fam["nbv"][j][:],
            scale=fam["rstd"][j][:]),
            reads=[src_key, fk(fam, "nbv", j), fk(fam, "rstd", j)],
            writes=[dst_key], tag=tagp)

    def transpose_to_lnxt(tile_t, tile_key, tc, tagp):
        for dc in range(NDC):
            p.op("pe", lambda e, dc=dc, tile_t=tile_t: e.matmul(
                psO[:, dc * 128:(dc + 1) * 128],
                tile_t[:, dc * 128:(dc + 1) * 128], idn[:],
                start=True, stop=True),
                reads=[tile_key, "idn"], writes=["psO"], tag=f"{tagp}_t{dc}")
        p.op("act", lambda e, tc=tc: e.activation(
            lnxt[:].rearrange("p (dc t) -> p dc t", dc=NDC)[:, :, tc * 128:(tc + 1) * 128],
            psO[:, :].rearrange("p (dc f) -> p dc f", dc=NDC), AF.Identity),
            reads=["psO"], writes=[("lnxt", tc)], tag=f"{tagp}_ev")

    def ph1(b, preps=()):
        preps = list(preps)

        def n1(tc):
            xti, xkey = xt[tc % 3], f"xt{tc % 3}"
            p.dma("sync", lambda e: e.dma_start(
                xti[:], x_d[b, tc * 128:(tc + 1) * 128, :]),
                writes=[xkey], tag=f"p1b{b}_ld{tc}")
            ln_stats(xti, xkey, fam1, tc % 2, f"p1b{b}t{tc}")

        def n2(tc):
            xti, xkey = xt[tc % 3], f"xt{tc % 3}"
            ln_apply(lnb[tc % 2], f"lnb{tc % 2}", xti, xkey, fam1, tc % 2,
                     f"p1b{b}t{tc}_ap")
            transpose_to_lnxt(lnb[tc % 2], f"lnb{tc % 2}", tc, f"p1b{b}t{tc}")

        for i in range(NTC + 1):
            if 1 <= i:
                n2(i - 1)
            if i < NTC:
                n1(i)
            npop = -(-len(preps) // (NTC + 1 - i)) if preps else 0
            while npop > 0 and preps and getattr(preps[0], "min_i", 0) <= i:
                preps.pop(0)()
                npop -= 1
        for th in preps:
            th()

    psX = [psO, psM]
    PSX = ["psO", "psM"]

    def proj_thunks(fc, b, host_ps, alt_qk=None):
        par = fc % 2
        out = [[], [], [], []]

        def qk_group(u, dst_t, dst_key, slot_t, slot_key, half, tg,
                     colf=None, rkey=None):
            ps = psX[host_ps[u]]
            psk = PSX[host_ps[u]]
            if colf is None:
                colf = lambda dc: dc * D + fc * 128
                rkey = (slot_key, fc)
            dcs = list(range(0, fc + 1)) if tg == "q" else list(range(fc, NDC))
            for qtr in range(2):
                for c0 in range(0, len(dcs), 2):
                    pair = dcs[c0:c0 + 2]
                    def th(e=None, pair=pair, qtr=qtr, ps=ps, psk=psk,
                           slot_t=slot_t, half=half, tg=tg, colf=colf,
                           rkey=rkey, dcs=dcs):
                        for dc in pair:
                            p.op("pe", lambda e, dc=dc, qtr=qtr, ps=ps,
                                 slot_t=slot_t, half=half: e.matmul(
                                     ps[:, qtr * 512:(qtr + 1) * 512],
                                     slot_t[:, colf(dc):colf(dc) + 128],
                                     lnxt[:, dc * T + (half * 2 + qtr) * 512:
                                          dc * T + (half * 2 + qtr) * 512 + 512],
                                     start=(dc == dcs[0]), stop=(dc == dcs[-1])),
                                 reads=[rkey] + [("lnxt", 4 * (half * 2 + qtr) + q)
                                                 for q in range(4)],
                                 writes=[psk],
                                 tag=f"p2b{b}f{fc}{tg}{half}{qtr}_{dc}")
                    th.min_i = 4 * (half * 2 + qtr) + 4
                    out[u].append(th)

                def ev(e=None, ps=ps, psk=psk, dst_t=dst_t, dst_key=dst_key,
                       half=half, qtr=qtr, tg=tg):
                    p.op("dve", lambda e: e.tensor_copy(
                        dst_t[:, (half * 2 + qtr) * 512:(half * 2 + qtr) * 512 + 512],
                        ps[:, qtr * 512:(qtr + 1) * 512]),
                        reads=[psk], writes=[(dst_key, half * 2 + qtr)],
                        tag=f"p2b{b}f{fc}{tg}e{half}{qtr}")
                ev.min_i = 4 * (half * 2 + qtr) + 4
                out[u].append(ev)

        def v_group(u, half):
            ps = psX[host_ps[u]]
            psk = PSX[host_ps[u]]
            for tcl in range(8):
                tc = half * 8 + tcl
                for dc0 in range(0, NDC, 4):
                    def th(e=None, tcl=tcl, tc=tc, dc0=dc0, ps=ps, psk=psk):
                        for dc in range(dc0, dc0 + 4):
                            p.op("pe", lambda e, tcl=tcl, tc=tc, dc=dc, ps=ps: e.matmul(
                                ps[:, tcl * 128:(tcl + 1) * 128],
                                lnxt[:, dc * T + tc * 128:dc * T + (tc + 1) * 128],
                                slotC[:, dc * D + fc * 128:dc * D + (fc + 1) * 128],
                                start=(dc == 0), stop=(dc == NDC - 1)),
                                reads=[("lnxt", tc), ("slotC", fc)], writes=[psk],
                                tag=f"p2b{b}f{fc}v{tc}_{dc}")
                    th.min_i = tc + 1
                    out[u].append(th)

                def ev(e=None, ps=ps, psk=psk, tcl=tcl, tc=tc):
                    p.op("dve", lambda e: e.tensor_copy(
                        vp[par][:].rearrange("p (tc lh c) -> p tc lh c",
                                             tc=NTC, lh=2)[:, tc, :, 0:64],
                        ps[:, tcl * 128:(tcl + 1) * 128]
                        .rearrange("p (lh s) -> p lh s", lh=2)),
                        reads=[psk], writes=[(f"vp{par}", tc)],
                        tag=f"p2b{b}f{fc}ve{tc}")
                ev.min_i = tc + 1
                out[u].append(ev)

        if alt_qk is None:
            qk_group(0, qf[par], f"qf{par}", slotB, "slotB", 0, "q")
            qk_group(1, qf[par], f"qf{par}", slotB, "slotB", 1, "q")
            qk_group(0, kf[par], f"kf{par}", slotA, "slotA", 0, "k")
            qk_group(1, kf[par], f"kf{par}", slotA, "slotA", 1, "k")
        else:
            mt, bq, bk, kq, kk = alt_qk
            cq = lambda dc: bq + dc * 128
            ck = lambda dc: bk + dc * 128
            qk_group(0, qf[par], f"qf{par}", mt, None, 0, "q", cq, kq)
            qk_group(1, qf[par], f"qf{par}", mt, None, 1, "q", cq, kq)
            qk_group(0, kf[par], f"kf{par}", mt, None, 0, "k", ck, kk)
            qk_group(1, kf[par], f"kf{par}", mt, None, 1, "k", ck, kk)
        v_group(2, 0)
        v_group(3, 1)
        return out

    NU = NFC * NQW

    def emit_s_pair(b, uidx, kc):
        fc, qw = divmod(uidx, NQW)
        par = fc % 2
        q0 = qw * 512
        sp = kc % 2
        p.op("pe", lambda e: e.matmul(
            psS[sp][:, 0:512], kf[par][0:64, kc * 128:(kc + 1) * 128],
            qf[par][0:64, q0:q0 + 512], start=True, stop=True,
            tile_position=(0, 0)),
            reads=[(f"kf{par}", kc // 4), (f"qf{par}", qw)],
            writes=[PSS[sp]], tag=f"u{uidx}sA{kc}")
        p.op("pe", lambda e: e.matmul(
            psS[sp][:, 512:1024], kf[par][64:128, kc * 128:(kc + 1) * 128],
            qf[par][64:128, q0:q0 + 512], start=True, stop=True,
            tile_position=(64, 0)),
            reads=[(f"kf{par}", kc // 4), (f"qf{par}", qw)],
            writes=[PSS[sp]], tag=f"u{uidx}sB{kc}")
        p.op("act", lambda e: e.activation(
            pt[sp][:, :], psS[sp][:, :], AF.Exp, scale=SCALE),
            reads=[PSS[sp]], writes=[f"pt{sp}"], tag=f"u{uidx}x{kc}")

    def emit_av(b, uidx, kc):
        fc, qw = divmod(uidx, NQW)
        par = fc % 2
        uc = uidx % 2
        psA = psX[uc]
        psk = PSX[uc]
        sp = kc % 2
        p.op("pe", lambda e: e.matmul(
            psA[0:65, 0:512], vp[par][:, kc * 130:kc * 130 + 65],
            pt[sp][:, 0:512], start=(kc == 0), stop=(kc == KCN - 1),
            tile_position=(0, 0)),
            reads=[(f"vp{par}", kc), f"pt{sp}"], writes=[psk],
            tag=f"u{uidx}aA{kc}")
        p.op("pe", lambda e: e.matmul(
            psA[0:65, 512:1024], vp[par][:, kc * 130 + 65:kc * 130 + 130],
            pt[sp][:, 512:1024], start=(kc == 0), stop=(kc == KCN - 1),
            tile_position=(0, 0)),
            reads=[(f"vp{par}", kc), f"pt{sp}"], writes=[psk],
            tag=f"u{uidx}aB{kc}")

    def emit_tail(b, uidx):
        fc, qw = divmod(uidx, NQW)
        uc = uidx % 2
        psA = psX[uc]
        psk = PSX[uc]
        q0 = qw * 512
        p.op("act", lambda e: e.activation(fbuf[0][64:65, :], psA[64:65, :], AF.Ln),
             reads=[psk], writes=["fbuf0"], tag=f"u{uidx}rc1")
        p.op("act", lambda e: e.activation(fbuf[0][64:65, :], fbuf[0][64:65, :],
                                           AF.Exp, scale=-1.0),
             reads=["fbuf0"], writes=["fbuf0"], tag=f"u{uidx}rc2")
        p.dma("pool", lambda e: e.dma_start(rrd[uc:uc + 1, :], fbuf[0][64:65, :]),
              reads=["fbuf0"], writes=[f"rrd{uc}"], tag=f"u{uidx}rd")
        p.dma("pool", lambda e: e.dma_start(
            rbsb[:, 0:512],
            bass.AP(rrd_t, uc * 1024, [[0, 64], [1, 512]])),
            reads=[f"rrd{uc}"], writes=["rbsbA"], tag=f"u{uidx}ba")
        p.dma("pool", lambda e: e.dma_start(
            rbsb[:, 512:1024],
            bass.AP(rrd_t, uc * 1024 + 512, [[0, 64], [1, 512]])),
            reads=[f"rrd{uc}"], writes=["rbsbB"], tag=f"u{uidx}bb")
        c0 = fc * T + q0
        p.op("dve", lambda e: e.tensor_mul(
            aofm[0:64, c0:c0 + 512], psA[0:64, 0:512],
            rbsb[:, 0:512]),
            reads=[psk, "rbsbA"], writes=[("aofm", uidx, "A")], tag=f"u{uidx}nA")
        p.op("dve", lambda e: e.tensor_mul(
            stg2[:, uc * 512:(uc + 1) * 512], psA[0:64, 512:1024],
            rbsb[:, 512:1024]),
            reads=[psk, "rbsbB"], writes=[f"stg2{uc}"], tag=f"u{uidx}nB")
        p.dma("pool", lambda e: e.dma_start(
            aofm[64:128, c0:c0 + 512], stg2[:, uc * 512:(uc + 1) * 512]),
            reads=[f"stg2{uc}"], writes=[("aofm", uidx, "B")], tag=f"u{uidx}sh")

    def stats_thunks(bn, tagp):
        ths = []
        for tc in range(NTC):
            def th(tc=tc):
                xti, xkey = xt[tc % 3], f"xt{tc % 3}"
                p.dma("sync", lambda e: e.dma_start(
                    xti[:], x_d[bn, tc * 128:(tc + 1) * 128, :]),
                    writes=[xkey], tag=f"{tagp}_ld{tc}")
                p.op("dve", lambda e: e.tensor_reduce(
                    sxal[:, tc:tc + 1], xti[:], AX.X, ALU.add),
                    reads=[xkey], writes=[("sxa", tc)], tag=f"{tagp}_sx{tc}")
                p.op("dve", lambda e: e.scalar_tensor_tensor(
                    jnk[:], xti[:], 1.0, xti[:], ALU.mult, ALU.mult,
                    accum_out=sqal[:, tc:tc + 1]),
                    reads=[xkey], writes=["jnk", ("sqa", tc)],
                    tag=f"{tagp}_sq{tc}")
            ths.append(th)

        def tail():
            rd = [("sxa", t) for t in range(NTC)] + [("sqa", t) for t in range(NTC)]
            p.op("dve", lambda e: e.tensor_scalar(
                sxal[:, :], sxal[:, :], INVD, None, ALU.mult),
                reads=rd[:NTC], writes=[("sxa", t) for t in range(NTC)],
                tag=f"{tagp}_m")
            p.op("dve", lambda e: e.tensor_scalar(
                sqal[:, :], sqal[:, :], INVD, EPS, ALU.mult, ALU.add),
                reads=rd[NTC:], writes=[("sqa", t) for t in range(NTC)],
                tag=f"{tagp}_q")
            p.op("dve", lambda e: e.tensor_mul(
                rstash[:, :], sxal[:, :], sxal[:, :]),
                reads=rd[:NTC], writes=["rstash"], tag=f"{tagp}_t1s")
            p.op("dve", lambda e: e.tensor_sub(
                rstash[:, :], sqal[:, :], rstash[:, :]),
                reads=rd[NTC:] + ["rstash"], writes=["rstash"], tag=f"{tagp}_v")
            p.op("act", lambda e: e.activation(
                rstash[:, :], rstash[:, :], AF.Ln),
                reads=["rstash"], writes=["rstash"], tag=f"{tagp}_ln")
            p.op("act", lambda e: e.activation(
                rstash[:, :], rstash[:, :], AF.Exp, scale=-0.5),
                reads=["rstash"], writes=["rstash"], tag=f"{tagp}_ex")
            p.op("dve", lambda e: e.tensor_scalar(
                nstash[:, :], sxal[:, :], -1.0, None, ALU.mult),
                reads=rd[:NTC], writes=["nstash"], tag=f"{tagp}_nm")
            p.op("dve", lambda e: e.tensor_mul(
                nstash[:, :], nstash[:, :], rstash[:, :]),
                reads=["nstash", "rstash"], writes=["nstash"], tag=f"{tagp}_nb")
        ths.append(tail)
        return ths

    def ph2(b, boundary_preps=()):
        bpreps = list(boundary_preps)
        for j in range(2):
            p.op("dve", lambda e, j=j: e.memset(
                vp[j][:].rearrange("p (tc lh c) -> p tc lh c", tc=NTC, lh=2)
                [:, :, :, 64:65], 1.0),
                writes=[(f"vp{j}", t) for t in range(NTC)],
                tag=f"vones{b}{j}")
        fillers = {u: [] for u in range(NU)}
        for fc in range(NFC - 1):
            u0 = fc * NQW
            host = [(u0 + u + 1) % 2 for u in range(4)]
            for u, grp in enumerate(proj_thunks(fc + 1, b, host_ps=host)):
                fillers[u0 + u] = grp
        FILL_START = 8
        emit_s_pair(b, 0, 0)
        for g in range(NU * KCN):
            uidx, kc = divmod(g, KCN)
            if g + 1 < NU * KCN:
                u2, kc2 = divmod(g + 1, KCN)
                emit_s_pair(b, u2, kc2)
            rem = fillers[uidx]
            if rem and kc >= FILL_START:
                npop = -(-len(rem) // (KCN - kc))
                for _ in range(min(npop, len(rem))):
                    rem.pop(0)()
            emit_av(b, uidx, kc)
            if kc == KCN - 1:
                for th in rem:
                    th()
                rem.clear()
                emit_tail(b, uidx)
                npp = 0
                nppmax = 8 if uidx >= 26 else 3
                while bpreps and bpreps[0][0] <= uidx and npp < nppmax:
                    bpreps.pop(0)[1]()
                    npp += 1
        for mu, th in bpreps:
            th()

    def ph34(b, extra=()):
        bn = b + 1

        def s1(tc):
            for half in range(2):
                for ec in range(NDC):
                    p.op("pe", lambda e, half=half, ec=ec: e.matmul(
                        psS[0][:, half * 512:(half + 1) * 512],
                        aofm[:, ec * T + tc * 128:ec * T + (tc + 1) * 128],
                        slotA[:, ec * D + half * 512:ec * D + (half + 1) * 512],
                        start=(ec == 0), stop=False),
                        reads=[("aofm", ec * 4 + tc // 4, "A"),
                               ("aofm", ec * 4 + tc // 4, "B")]
                        + [("slotA", half * 4 + k) for k in range(4)],
                        writes=["psS0"], tag=f"p3b{b}t{tc}u{half}_{ec}")
                p.op("pe", lambda e, half=half: e.matmul(
                    psS[0][:, half * 512:(half + 1) * 512], onesbr[:],
                    bub[0:1, half * 512:(half + 1) * 512],
                    start=False, stop=True),
                    reads=["onesbr", "bub"], writes=["psS0"],
                    tag=f"p3b{b}t{tc}ub{half}")

        def s2(tc):
            j = tc % 2
            yti, ykey = yt[j], f"yt{j}"
            p.op("act", lambda e: e.activation(
                yti[:], psS[0][:, :], AF.Identity, accum_out=fam1["sx"][j][:]),
                reads=["psS0"], writes=[ykey, fk(fam1, "sx", j)],
                tag=f"p3b{b}t{tc}ev")
            p.op("act", lambda e: e.activation(
                psS[0][:, :], psS[0][:, :], AF.Square, accum_out=fam1["sq"][j][:]),
                reads=["psS0"], writes=["psS0", fk(fam1, "sq", j)],
                tag=f"p3b{b}t{tc}sq")

        def s3(tc):
            j = tc % 2
            xti, xkey = xt[tc % 3], f"xt{tc % 3}"
            stats_tail(fam1, j, f"p3b{b}t{tc}")
            p.dma("sync", lambda e: e.dma_start(
                xti[:], x_d[b, tc * 128:(tc + 1) * 128, :]),
                writes=[xkey], tag=f"p3b{b}_ld{tc}")

        def s45a(tc):
            j = tc % 2
            xti, xkey = xt[tc % 3], f"xt{tc % 3}"
            yti, ykey = yt[j], f"yt{j}"
            p.op("dve", lambda e: e.tensor_scalar(
                xti[:], xti[:], fam1["nbv"][j][:], None, ALU.add),
                reads=[xkey, fk(fam1, "nbv", j)], writes=[xkey],
                tag=f"p3b{b}t{tc}xnb")
            p.op("dve", lambda e: e.scalar_tensor_tensor(
                xti[:], yti[:], fam1["rstd"][j][:], xti[:], ALU.mult, ALU.add,
                accum_out=fam2["sx"][j][:]),
                reads=[xkey, ykey, fk(fam1, "rstd", j)],
                writes=[xkey, fk(fam2, "sx", j)], tag=f"p3b{b}t{tc}res")
            p.dma("pool", lambda e: e.dma_start(
                xp_d[tc * 128:(tc + 1) * 128, :], xti[:]),
                reads=[xkey], writes=[("xp", tc)], tag=f"p3b{b}_bo{tc}")

        def s45b(tc):
            j = tc % 2
            xti, xkey = xt[tc % 3], f"xt{tc % 3}"
            p.op("act", lambda e: e.activation(
                psO[:, :], xti[:], AF.Square, accum_out=fam2["sq"][j][:]),
                reads=[xkey], writes=["psO", fk(fam2, "sq", j)],
                tag=f"p3b{b}t{tc}sq2")
            stats_tail(fam2, j, f"p3Bb{b}t{tc}", with_nb=False)
            p.op("dve", lambda e: e.tensor_scalar(
                lnb[j][:], xti[:], fam2["t1"][j][:], fam2["rstd"][j][:],
                ALU.subtract, ALU.mult),
                reads=[xkey, fk(fam2, "t1", j), fk(fam2, "rstd", j)],
                writes=[f"lnb{j}"], tag=f"p3b{b}t{tc}_l2")

        def s6(tc):
            j = tc % 2
            transpose_to_lnxt(lnb[j], f"lnb{j}", tc, f"p3b{b}t{tc}")
            for half in range(2):
                for dc in range(NDC):
                    p.op("pe", lambda e, half=half, dc=dc: e.matmul(
                        psS[1][:, half * 512:(half + 1) * 512],
                        lnxt[:, dc * T + tc * 128:dc * T + (tc + 1) * 128],
                        slotB[:, dc * D + half * 512:dc * D + (half + 1) * 512],
                        start=(dc == 0), stop=(dc == NDC - 1)),
                        reads=[("lnxt", tc)] + [("slotB", half * 4 + k) for k in range(4)],
                        writes=["psS1"],
                        tag=f"p4b{b}t{tc}f{half}_{dc}")

        def s78a(tc):
            j = tc % 2
            fti, fky = fbuf[j], f"fbuf{j}"
            x2 = xt[(tc + 1) % 3]
            x2key = f"xt{(tc + 1) % 3}"
            p.op("act", lambda e: e.activation(
                fti[:], psS[1][:, :], AF.Identity, accum_out=fam3["sx"][j][:]),
                reads=["psS1"], writes=[fky, fk(fam3, "sx", j)],
                tag=f"p4b{b}t{tc}ev")
            p.dma("sync", lambda e: e.dma_start(
                x2[:], xp_d[tc * 128:(tc + 1) * 128, :]),
                reads=[("xp", tc)], writes=[x2key], tag=f"p4b{b}_ld{tc}")
            p.op("act", lambda e: e.activation(
                psS[1][:, :], psS[1][:, :], AF.Square, accum_out=fam3["sq"][j][:]),
                reads=["psS1"], writes=["psS1", fk(fam3, "sq", j)],
                tag=f"p4b{b}t{tc}sq")

        def s78b(tc):
            j = tc % 2
            fti, fky = fbuf[j], f"fbuf{j}"
            x2 = xt[(tc + 1) % 3]
            x2key = f"xt{(tc + 1) % 3}"
            stats_tail(fam3, j, f"p4b{b}t{tc}")
            p.op("dve", lambda e: e.tensor_scalar(
                x2[:], x2[:], fam3["nbv"][j][:], None, ALU.add),
                reads=[x2key, fk(fam3, "nbv", j)], writes=[x2key],
                tag=f"p4b{b}t{tc}xnb")
            p.op("dve", lambda e: e.scalar_tensor_tensor(
                x2[:], fti[:], fam3["rstd"][j][:], x2[:], ALU.mult, ALU.add),
                reads=[x2key, fky, fk(fam3, "rstd", j)], writes=[x2key],
                tag=f"p4b{b}t{tc}res")
            p.dma("pool", lambda e: e.dma_start(
                out_d[b, tc * 128:(tc + 1) * 128, :], x2[:]),
                reads=[x2key], tag=f"p4b{b}_st{tc}")

        def r1(tc):
            xti = fbuf[tc % 2]
            fky = f"fbuf{tc % 2}"
            p.dma("sync", lambda e: e.dma_start(
                xti[:], x_d[bn, tc * 128:(tc + 1) * 128, :]),
                writes=[fky], tag=f"r1b{bn}_ld{tc}")
            p.op("act", lambda e: e.activation(
                pt[tc % 2][:, :], xti[:], AF.Identity,
                bias=nstash[:, tc:tc + 1], scale=rstash[:, tc:tc + 1]),
                reads=[fky, "nstash", "rstash"], writes=[f"pt{tc % 2}"],
                tag=f"r1b{bn}_ap{tc}")

        def r2(tc):
            for dc in range(NDC):
                p.op("pe", lambda e, dc=dc: e.matmul(
                    psO[:, dc * 128:(dc + 1) * 128],
                    pt[tc % 2][:, dc * 128:(dc + 1) * 128], idn[:],
                    start=True, stop=True),
                    reads=[f"pt{tc % 2}", "idn"], writes=["psO"],
                    tag=f"r2b{bn}_t{tc}_{dc}")
            p.op("act", lambda e: e.activation(
                lnxt[:].rearrange("p (dc t) -> p dc t", dc=NDC)[:, :, tc * 128:(tc + 1) * 128],
                psO[:, :].rearrange("p (dc f) -> p dc f", dc=NDC), AF.Identity),
                reads=["psO"], writes=[("lnxt", tc)], tag=f"r2b{bn}_ev{tc}")

        extq = list(extra)
        nsteps = NTC + 6
        for i in range(nsteps):
            if 5 <= i < NTC + 5:
                s78a(i - 5)
            if 4 <= i < NTC + 4:
                s6(i - 4)
            if 3 <= i < NTC + 3:
                s45a(i - 3)
            if 1 <= i < NTC + 1:
                s2(i - 1)
            if i < NTC:
                s1(i)
            if 3 <= i < NTC + 3:
                s45b(i - 3)
            if 2 <= i < NTC + 2:
                s3(i - 2)
            if 5 <= i < NTC + 5:
                s78b(i - 5)
            if bn < NB:
                if 4 <= i < NTC + 4:
                    r1(i - 4)
                if 5 <= i < NTC + 5:
                    r2(i - 5)
                npop = 6
                while npop > 0 and extq and getattr(extq[0], "min_i", 0) <= i:
                    extq.pop(0)()
                    npop -= 1
        for th in extq:
            th()

    def disha_thunks(tagp):
        ths = []
        for dc in range(NDC):
            def th(dc=dc):
                p.op("dve", lambda e: e.tensor_add(
                    slotB[:, dc * D:dc * D + D], slotB[:, dc * D:dc * D + D],
                    d2[:]),
                    reads=["d2"] + [("slotB", k) for k in range(NDC)],
                    writes=[("slotB", k) for k in range(NDC)],
                    tag=f"{tagp}_{dc}")
            ths.append(th)
        return ths

    for b in range(NB):
        if b == 0:
            wkt = prep_weight_thunks(w_d["wk"], slotA, "slotA", mku_d, "pk0",
                                     tri="up", evac_eng="act")
            wqt = prep_weight_thunks(w_d["wq"], slotB, "slotB", mkl_d,
                                     "pq0", tri="low", widx=1, evac_eng="act")
            wvt = prep_weight_thunks(w_d["wv"], slotC, "slotC", None, "pv0",
                                     evac_eng="act")
            preps = []
            for fb in range(NDC):
                for th in (wkt[fb], wqt[fb], wvt[fb]):
                    th.min_i = fb
                    preps.append(th)
            preps.extend(t for grp in proj_thunks(0, 0, host_ps=[1, 1, 1, 1])
                         for t in grp)
            preps.sort(key=lambda t: getattr(t, "min_i", 0))
            ph1(0, preps)
        bp = ([(28 + i // 4, th) for i, th in
               enumerate(prep_weight_thunks(w_d["wf"], slotB, "slotB",
                                            None, f"pf{b}"))]
              + [(30, th) for th in disha_thunks(f"pd{b}")]
              + [(26 + i // 4, th) for i, th in
                 enumerate(prep_weight_thunks(w_d["wu"], slotA,
                                              "slotA", None, f"pu{b}"))])
        if b + 1 < NB:
            bp += [(13 + tc, th) for tc, th in
                   enumerate(stats_thunks(b + 1, f"s1b{b + 1}"))]
            bp += [(30, mini_prep_thunk(w_d["wk"], mku_d, 0, "miniK",
                                        f"mk{b + 1}")),
                   (31, mini_prep_thunk(w_d["wq"], mkl_d, D, "miniQ",
                                        f"mq{b + 1}", ndc=1))]
        bp.sort(key=lambda x: x[0])
        ph2(b, bp)
        extra = []
        if b + 1 < NB:
            wkp = prep_weight_thunks(w_d["wk"], slotA, "slotA", mku_d,
                                     f"pk{b + 1}", tri="up")
            wqp = prep_weight_thunks(w_d["wq"], slotB, "slotB", mkl_d,
                                     f"pq{b + 1}", tri="low")
            for th in wkp:
                th.min_i = NTC
            for th in wqp:
                th.min_i = NTC + 4
            grps = proj_thunks(0, b + 1, host_ps=[1, 1, 1, 1],
                               alt_qk=(mini, D, 0, "miniQ", "miniK"))
            for grp in grps:
                for th in grp:
                    th.min_i = getattr(th, "min_i", 0) + 5
            extra = sorted(grps[0] + grps[1] + grps[2] + grps[3],
                           key=lambda t: t.min_i)
            extra += wkp + wqp
        ph34(b, extra)

    with es:
        p.emit(nc)
    return nc


_NC_CACHE = None


def kernel(L0_attn_x, L0_attn_Wk, L0_attn_Wq, L0_attn_Wv, L0_attn_Wu,
           L0_attn_bu, L0_ffn_weight, L0_ffn_disha):
    global _NC_CACHE
    x = np.ascontiguousarray(np.asarray(L0_attn_x, dtype=np.float32))
    wk = np.ascontiguousarray(np.asarray(L0_attn_Wk, dtype=np.float32))
    wq = np.ascontiguousarray(np.asarray(L0_attn_Wq, dtype=np.float32))
    wv = np.ascontiguousarray(np.asarray(L0_attn_Wv, dtype=np.float32))
    wu = np.ascontiguousarray(np.asarray(L0_attn_Wu, dtype=np.float32))
    bu = np.ascontiguousarray(np.asarray(L0_attn_bu, dtype=np.float32)).reshape(1, D)
    wf = np.ascontiguousarray(np.asarray(L0_ffn_weight, dtype=np.float32))
    dh = np.ascontiguousarray(np.asarray(L0_ffn_disha, dtype=np.float32))
    mku = np.triu(np.ones((D, D), dtype=np.float32)).astype(ml_dtypes.bfloat16)
    mkl = np.tril(np.ones((D, D), dtype=np.float32)).astype(ml_dtypes.bfloat16)
    idn = np.eye(128, dtype=np.float32).astype(ml_dtypes.bfloat16)

    if _NC_CACHE is None:
        _NC_CACHE = build_nc()
    nc = _NC_CACHE

    in_maps = []
    for c in range(8):
        in_maps.append({
            "x": x[2 * c:2 * c + 2],
            "wk": wk, "wq": wq, "wv": wv, "wu": wu, "wf": wf,
            "bu": bu, "disha": dh, "mku": mku, "mkl": mkl, "idn": idn,
        })
    res = run_bass_kernel_spmd(nc, in_maps, core_ids=list(range(8)))
    return np.concatenate([np.asarray(r["out"]) for r in res.results], axis=0)


# version 42
# speedup vs baseline: 1.0236x; 1.0236x over previous
import numpy as np
import ml_dtypes
import concourse.bass as bass
import concourse.mybir as mybir
from concourse.bass_utils import run_bass_kernel_spmd

F32 = mybir.dt.float32
BF16 = mybir.dt.bfloat16
AF = mybir.ActivationFunctionType
ALU = mybir.AluOpType
AX = mybir.AxisListType

ENGINES = ("pe", "act", "dve", "pool", "sync")
BLOCK_ATTR = {"pe": "tensor", "act": "scalar", "dve": "vector",
              "pool": "gpsimd", "sync": "sync"}
NCHAN = 8


class _Op:
    __slots__ = ("engine", "fn", "reads", "writes", "pos", "is_dma",
                 "need_inc", "waits", "tag", "chan", "seq")

    def __init__(self, engine, fn, reads, writes, is_dma, tag):
        self.engine = engine
        self.fn = fn
        self.reads = reads
        self.writes = writes
        self.is_dma = is_dma
        self.pos = -1
        self.need_inc = False
        self.waits = []
        self.tag = tag
        self.chan = None
        self.seq = 0


class Prog:
    def __init__(self):
        self.ops = {e: [] for e in ENGINES}
        self.regions = {}
        self._dma_rr = {}
        self._chan_last = {}

    def op(self, engine, fn, reads=(), writes=(), tag=""):
        return self._add(_Op(engine, fn, tuple(reads), tuple(writes), False, tag))

    def dma(self, queue, fn, reads=(), writes=(), tag="", chan=None):
        o = _Op(queue, fn, tuple(reads), tuple(writes), True, tag)
        if chan is None:
            chan = self._dma_rr.get(queue, 0) % NCHAN
            self._dma_rr[queue] = self._dma_rr.get(queue, 0) + 1
        key = (queue, chan)
        prev = self._chan_last.get(key)
        o.chan = key
        o.seq = (prev.seq + 1) if prev is not None else 1
        self._chan_last[key] = o
        if prev is not None:
            o.waits.append(prev)
        return self._add(o)

    def _add(self, o):
        o.pos = len(self.ops[o.engine])
        self.ops[o.engine].append(o)
        for r in o.reads:
            w, readers = self.regions.get(r, (None, []))
            if w is not None:
                self._edge(w, o)
            readers.append(o)
            self.regions[r] = (w, readers)
        for r in o.writes:
            w, readers = self.regions.get(r, (None, []))
            if w is not None:
                self._edge(w, o)
            for rd in readers:
                if rd is not o:
                    self._edge(rd, o)
            self.regions[r] = (o, [])
        return o

    def _edge(self, producer, consumer):
        if producer is consumer:
            return
        if (producer.engine == consumer.engine and producer.engine == "pe"
                and not producer.is_dma and not consumer.is_dma):
            return
        producer.need_inc = True
        consumer.waits.append(producer)

    def emit(self, nc):
        sem_val_after = {}
        counters = {}
        for e in ENGINES:
            for o in self.ops[e]:
                if o.is_dma:
                    key = f"dma_{o.chan[0]}_{o.chan[1]}"
                    counters[key] = counters.get(key, 0) + 16
                    sem_val_after[o] = (key, 16 * o.seq)
                elif o.need_inc:
                    key = "eng_" + e
                    counters[key] = counters.get(key, 0) + 1
                    sem_val_after[o] = (key, counters[key])
        import contextlib
        es = contextlib.ExitStack()
        sems = {k: es.enter_context(nc.semaphore(k)) for k in counters}
        with es, nc.Block() as block:
            for e in ENGINES:
                if not self.ops[e]:
                    continue

                def make_body(eng_name):
                    def body(eng):
                        highwater = {}
                        for o in self.ops[eng_name]:
                            needed = {}
                            for prod in o.waits:
                                if prod not in sem_val_after:
                                    raise RuntimeError(
                                        f"dep of {o.tag} on {prod.tag} lost")
                                k, v = sem_val_after[prod]
                                if needed.get(k, 0) < v:
                                    needed[k] = v
                            for k, v in needed.items():
                                if highwater.get(k, 0) < v:
                                    eng.wait_ge(sems[k], v)
                                    highwater[k] = v
                            inst = o.fn(eng)
                            if o in sem_val_after:
                                k, _ = sem_val_after[o]
                                inst.then_inc(sems[k], 16 if o.is_dma else 1)
                        for k, v in counters.items():
                            if highwater.get(k, 0) < v:
                                eng.wait_ge(sems[k], v)
                    return body

                getattr(block, BLOCK_ATTR[e])(make_body(e))


NB = 2
T = 2048
D = 1024
NTC = 16
NDC = 8
NFC = 8
NQW = 4
KCN = 16
EPS = 1e-5
SCALE = 0.125
INVD = 1.0 / D


def build_nc():
    nc = bass.Bass("TRN2", target_bir_lowering=False)
    x_d = nc.dram_tensor("x", [NB, T, D], F32, kind="ExternalInput").ap()
    w_d = {k: nc.dram_tensor(k, [D, D], F32, kind="ExternalInput").ap()
           for k in ("wk", "wq", "wv", "wu", "wf")}
    bu_d = nc.dram_tensor("bu", [1, D], F32, kind="ExternalInput").ap()
    disha_d = nc.dram_tensor("disha", [64, D], F32, kind="ExternalInput").ap()
    mku_d = nc.dram_tensor("mku", [D, D], BF16, kind="ExternalInput").ap()
    mkl_d = nc.dram_tensor("mkl", [D, D], BF16, kind="ExternalInput").ap()
    idn_d = nc.dram_tensor("idn", [128, 128], BF16, kind="ExternalInput").ap()
    out_d = nc.dram_tensor("out", [NB, T, D], F32, kind="ExternalOutput").ap()
    xp_d = nc.dram_tensor("xp", [T, D], F32, kind="Internal").ap()
    rrd_t = nc.dram_tensor("rrd", [2, 1024], F32, kind="Internal")
    rrd = rrd_t.ap()

    import contextlib
    es = contextlib.ExitStack()
    sb = lambda n, sh, dt: es.enter_context(nc.sbuf_tensor(n, sh, dt))
    pst = lambda n: es.enter_context(nc.psum_tensor(n, [128, 1024], F32))

    lnxt = sb("lnxt", [128, NDC * T], BF16)
    aofm = sb("aofm", [128, NFC * T], BF16)
    slotA = sb("slotA", [128, NDC * D], BF16)
    slotB = sb("slotB", [128, NDC * D], BF16)
    slotC = sb("slotC", [128, NDC * D], BF16)
    qf = [sb("qf0", [128, T], BF16), sb("qf1", [128, T], BF16)]
    kf = [sb("kf0", [128, T], BF16), sb("kf1", [128, T], BF16)]
    vp = [sb("vp0", [128, NTC * 130], BF16), sb("vp1", [128, NTC * 130], BF16)]
    pt = [sb("pt0", [128, 1024], BF16), sb("pt1", [128, 1024], BF16)]
    xt = [sb(f"xt{i}", [128, D], F32) for i in range(3)]
    yt = [sb(f"yt{i}", [128, D], F32) for i in range(2)]
    jnk = sb("jnk", [128, D], F32)
    lnb = [sb("lnb0", [128, D], BF16), sb("lnb1", [128, D], BF16)]
    rbsb = sb("rbsb", [64, 1024], F32)
    rstash = sb("rstash", [128, NTC], F32)
    nstash = sb("nstash", [128, NTC], F32)
    sxal = sb("sxal", [128, NTC], F32)
    sqal = sb("sqal", [128, NTC], F32)
    mini = sb("mini", [128, 2 * D], BF16)
    bub = sb("bub", [1, D], BF16)
    onesbr = sb("onesbr", [1, 128], BF16)
    stg2 = sb("stg2", [64, 1024], BF16)
    fbuf = [sb("fbuf0", [128, D], F32), sb("fbuf1", [128, D], F32)]
    d2 = sb("d2", [128, D], BF16)
    onescf = sb("onescf", [1, 128], F32)
    ones128 = sb("ones128", [128, 1], BF16)
    idn = sb("idn_s", [128, 128], BF16)
    wraw = [sb("wraw0", [128, D], F32), sb("wraw1", [128, D], F32)]
    wc2 = sb("wc2", [128, D], BF16)
    mkt = [sb("mkt0", [128, D], BF16)] * 2
    dstg = jnk

    def make_fam(pfx):
        names = ("sx", "sq", "t1", "q1", "t1s", "var", "lnv", "rstd", "nbv")
        return {"pfx": pfx, **{n: [sb(f"{pfx}{n}{j}", [128, 1], F32)
                                   for j in range(2)] for n in names}}
    fam1 = make_fam("fa")
    fam2 = make_fam("fb")
    fam3 = make_fam("fc")

    def fk(fam, n, j):
        return f"{fam['pfx']}{n}{j}"

    psS = [pst("psS0"), pst("psS1")]
    psO = pst("psO")
    psM = pst("psM")
    PSS = ["psS0", "psS1"]

    p = Prog()

    p.dma("sync", lambda e: e.dma_start(idn[:], idn_d), writes=["idn"], tag="ld_idn")
    p.op("dve", lambda e: e.memset(onescf[:], 1.0), writes=["onescf"], tag="ones_f")
    p.op("dve", lambda e: e.memset(ones128[:], 1.0), writes=["ones128"], tag="ones_b")
    p.dma("sync", lambda e: e.dma_start(dstg[0:1, :], bu_d), writes=["jnk"], tag="ld_bu")
    p.op("act", lambda e: e.copy(bub[:], dstg[0:1, :]), reads=["jnk"],
         writes=["bub"], tag="bu_b16")
    p.op("dve", lambda e: e.memset(onesbr[:], 1.0), writes=["onesbr"], tag="ones_br")
    p.dma("sync", lambda e: e.dma_start(dstg[0:64, :], disha_d), writes=["jnk"], tag="ld_d0")
    p.dma("sync", lambda e: e.dma_start(dstg[64:128, :], disha_d), writes=["jnk"], tag="ld_d1")
    p.op("act", lambda e: e.copy(d2[:], dstg[:]), reads=["jnk"], writes=["d2"], tag="d2c")

    def prep_weight_thunks(wd, slot_t, slot_key, mask_d, tagp, tri=None,
                           widx=0, evac_eng="dve"):
        ths = []
        for fb in range(NDC):
            def th(fb=fb):
                if tri == "low":
                    lo, hi = 0, fb + 1
                elif tri == "up":
                    lo, hi = fb, NDC
                else:
                    lo, hi = 0, NDC
                wr = wraw[(fb + widx) % 2]
                wrk = f"wraw{(fb + widx) % 2}"
                p.dma("sync", lambda e: e.dma_start(
                    wr[:, lo * 128:hi * 128],
                    wd[fb * 128:(fb + 1) * 128, lo * 128:hi * 128]),
                    writes=[wrk], tag=f"{tagp}_ld{fb}")
                p.op("dve", lambda e: e.tensor_copy(
                    wc2[:, lo * 128:hi * 128], wr[:, lo * 128:hi * 128]),
                    reads=[wrk], writes=["wc2"], tag=f"{tagp}_c{fb}")
                if mask_d is not None:
                    p.dma("sync", lambda e: e.dma_start(
                        mkt[0][:, 0:128],
                        mask_d[fb * 128:(fb + 1) * 128, fb * 128:(fb + 1) * 128]),
                        writes=["mkt0"], tag=f"{tagp}_ldm{fb}")
                    p.op("dve", lambda e: e.tensor_mul(
                        wc2[:, fb * 128:(fb + 1) * 128],
                        wc2[:, fb * 128:(fb + 1) * 128], mkt[0][:, 0:128]),
                        reads=["mkt0", "wc2"], writes=["wc2"], tag=f"{tagp}_m{fb}")
                for dc in range(lo, hi):
                    p.op("pe", lambda e, dc=dc: e.matmul(
                        psO[:, dc * 128:(dc + 1) * 128],
                        wc2[:, dc * 128:(dc + 1) * 128],
                        idn[:], start=True, stop=True),
                        reads=["wc2", "idn"], writes=["psO"],
                        tag=f"{tagp}_t{fb}_{dc}")
                sdst = lambda: slot_t[:].rearrange(
                    "p (dc f) -> p dc f", dc=NDC)[:, lo:hi, fb * 128:(fb + 1) * 128]
                ssrc = lambda: psO[:, :].rearrange(
                    "p (dc f) -> p dc f", dc=NDC)[:, lo:hi, :]
                if evac_eng == "act":
                    p.op("act", lambda e: e.activation(
                        sdst(), ssrc(), AF.Identity),
                        reads=["psO"], writes=[(slot_key, fb)],
                        tag=f"{tagp}_e{fb}")
                else:
                    p.op("dve", lambda e: e.tensor_copy(sdst(), ssrc()),
                         reads=["psO"], writes=[(slot_key, fb)],
                         tag=f"{tagp}_e{fb}")
            ths.append(th)
        return ths

    def mini_prep_thunk(wd, mask_d, base, mkey, tagp, ndc=NDC):
        def th():
            p.dma("sync", lambda e: e.dma_start(
                wraw[0][:, 0:ndc * 128], wd[0:128, 0:ndc * 128]),
                writes=["wraw0"], tag=f"{tagp}_ld")
            p.op("dve", lambda e: e.tensor_copy(
                wc2[:, 0:ndc * 128], wraw[0][:, 0:ndc * 128]),
                reads=["wraw0"], writes=["wc2"], tag=f"{tagp}_c")
            if mask_d is not None:
                p.dma("sync", lambda e: e.dma_start(
                    mkt[0][:, 0:128], mask_d[0:128, 0:128]), writes=["mkt0"],
                    tag=f"{tagp}_lm")
                p.op("dve", lambda e: e.tensor_mul(
                    wc2[:, 0:128], wc2[:, 0:128], mkt[0][:, 0:128]),
                    reads=["mkt0", "wc2"], writes=["wc2"], tag=f"{tagp}_m")
            for dc in range(ndc):
                p.op("pe", lambda e, dc=dc: e.matmul(
                    psO[:, dc * 128:(dc + 1) * 128],
                    wc2[:, dc * 128:(dc + 1) * 128],
                    idn[:], start=True, stop=True),
                    reads=["wc2", "idn"], writes=["psO"], tag=f"{tagp}_t{dc}")
            p.op("dve", lambda e: e.tensor_copy(
                mini[:, base:base + ndc * 128], psO[:, 0:ndc * 128]),
                reads=["psO"], writes=[mkey], tag=f"{tagp}_e")
        return th

    def weff_add_disha(tagp):
        for dc in range(NDC):
            p.op("dve", lambda e, dc=dc: e.tensor_add(
                slotB[:, dc * D:dc * D + D], slotB[:, dc * D:dc * D + D], d2[:]),
                reads=["slotB", "d2"], writes=["slotB"], tag=f"{tagp}_{dc}")

    def stats_tail(fam, j, tagp, with_nb=True, eng="dve"):
        p.op(eng, lambda e: e.tensor_scalar(
            fam["t1"][j][:], fam["sx"][j][:], INVD, None, ALU.mult),
            reads=[fk(fam, "sx", j)], writes=[fk(fam, "t1", j)], tag=f"{tagp}_t1")
        p.op(eng, lambda e: e.tensor_scalar(
            fam["q1"][j][:], fam["sq"][j][:], INVD, EPS, ALU.mult, ALU.add),
            reads=[fk(fam, "sq", j)], writes=[fk(fam, "q1", j)], tag=f"{tagp}_q1")
        p.op(eng, lambda e: e.tensor_scalar(
            fam["t1s"][j][:], fam["t1"][j][:], fam["t1"][j][:], None, ALU.mult),
            reads=[fk(fam, "t1", j)], writes=[fk(fam, "t1s", j)], tag=f"{tagp}_t1s")
        p.op(eng, lambda e: e.tensor_sub(
            fam["var"][j][:], fam["q1"][j][:], fam["t1s"][j][:]),
            reads=[fk(fam, "q1", j), fk(fam, "t1s", j)],
            writes=[fk(fam, "var", j)], tag=f"{tagp}_var")
        p.op("act", lambda e: e.activation(
            fam["lnv"][j][:], fam["var"][j][:], AF.Ln),
            reads=[fk(fam, "var", j)], writes=[fk(fam, "lnv", j)], tag=f"{tagp}_lnv")
        p.op("act", lambda e: e.activation(
            fam["rstd"][j][:], fam["lnv"][j][:], AF.Exp, scale=-0.5),
            reads=[fk(fam, "lnv", j)], writes=[fk(fam, "rstd", j)], tag=f"{tagp}_rstd")
        if with_nb:
            p.op(eng, lambda e: e.tensor_scalar(
                fam["nbv"][j][:], fam["t1"][j][:], fam["rstd"][j][:], -1.0,
                ALU.mult, ALU.mult),
                reads=[fk(fam, "t1", j), fk(fam, "rstd", j)],
                writes=[fk(fam, "nbv", j)], tag=f"{tagp}_nb")

    def ln_stats(src_t, src_key, fam, j, tagp):
        p.op("dve", lambda e: e.tensor_reduce(
            fam["sx"][j][:], src_t[:], AX.X, ALU.add),
            reads=[src_key], writes=[fk(fam, "sx", j)], tag=f"{tagp}_sx")
        p.op("dve", lambda e: e.scalar_tensor_tensor(
            jnk[:], src_t[:], 1.0, src_t[:], ALU.mult, ALU.mult,
            accum_out=fam["sq"][j][:]),
            reads=[src_key], writes=["jnk", fk(fam, "sq", j)], tag=f"{tagp}_sq")
        stats_tail(fam, j, tagp)

    def ln_apply(dst_t, dst_key, src_t, src_key, fam, j, tagp):
        p.op("act", lambda e: e.activation(
            dst_t[:], src_t[:], AF.Identity, bias=fam["nbv"][j][:],
            scale=fam["rstd"][j][:]),
            reads=[src_key, fk(fam, "nbv", j), fk(fam, "rstd", j)],
            writes=[dst_key], tag=tagp)

    def transpose_to_lnxt(tile_t, tile_key, tc, tagp):
        for dc in range(NDC):
            p.op("pe", lambda e, dc=dc, tile_t=tile_t: e.matmul(
                psO[:, dc * 128:(dc + 1) * 128],
                tile_t[:, dc * 128:(dc + 1) * 128], idn[:],
                start=True, stop=True),
                reads=[tile_key, "idn"], writes=["psO"], tag=f"{tagp}_t{dc}")
        p.op("act", lambda e, tc=tc: e.activation(
            lnxt[:].rearrange("p (dc t) -> p dc t", dc=NDC)[:, :, tc * 128:(tc + 1) * 128],
            psO[:, :].rearrange("p (dc f) -> p dc f", dc=NDC), AF.Identity),
            reads=["psO"], writes=[("lnxt", tc)], tag=f"{tagp}_ev")

    def ph1(b, preps=()):
        preps = list(preps)

        def n1(tc):
            xti, xkey = xt[tc % 3], f"xt{tc % 3}"
            p.dma("sync", lambda e: e.dma_start(
                xti[:], x_d[b, tc * 128:(tc + 1) * 128, :]),
                writes=[xkey], tag=f"p1b{b}_ld{tc}")
            ln_stats(xti, xkey, fam1, tc % 2, f"p1b{b}t{tc}")

        def n2(tc):
            xti, xkey = xt[tc % 3], f"xt{tc % 3}"
            ln_apply(lnb[tc % 2], f"lnb{tc % 2}", xti, xkey, fam1, tc % 2,
                     f"p1b{b}t{tc}_ap")
            transpose_to_lnxt(lnb[tc % 2], f"lnb{tc % 2}", tc, f"p1b{b}t{tc}")

        for i in range(NTC + 1):
            if 1 <= i:
                n2(i - 1)
            if i < NTC:
                n1(i)
            npop = -(-len(preps) // (NTC + 1 - i)) if preps else 0
            while npop > 0 and preps and getattr(preps[0], "min_i", 0) <= i:
                preps.pop(0)()
                npop -= 1
        for th in preps:
            th()

    psX = [psO, psM]
    PSX = ["psO", "psM"]

    def proj_thunks(fc, b, host_ps, alt_qk=None):
        par = fc % 2
        out = [[], [], [], []]

        def qk_group(u, dst_t, dst_key, slot_t, slot_key, half, tg,
                     colf=None, rkey=None):
            ps = psX[host_ps[u]]
            psk = PSX[host_ps[u]]
            if colf is None:
                colf = lambda dc: dc * D + fc * 128
                rkey = (slot_key, fc)
            dcs = list(range(0, fc + 1)) if tg == "q" else list(range(fc, NDC))
            for qtr in range(2):
                for c0 in range(0, len(dcs), 2):
                    pair = dcs[c0:c0 + 2]
                    def th(e=None, pair=pair, qtr=qtr, ps=ps, psk=psk,
                           slot_t=slot_t, half=half, tg=tg, colf=colf,
                           rkey=rkey, dcs=dcs):
                        for dc in pair:
                            p.op("pe", lambda e, dc=dc, qtr=qtr, ps=ps,
                                 slot_t=slot_t, half=half: e.matmul(
                                     ps[:, qtr * 512:(qtr + 1) * 512],
                                     slot_t[:, colf(dc):colf(dc) + 128],
                                     lnxt[:, dc * T + (half * 2 + qtr) * 512:
                                          dc * T + (half * 2 + qtr) * 512 + 512],
                                     start=(dc == dcs[0]), stop=(dc == dcs[-1])),
                                 reads=[rkey] + [("lnxt", 4 * (half * 2 + qtr) + q)
                                                 for q in range(4)],
                                 writes=[psk],
                                 tag=f"p2b{b}f{fc}{tg}{half}{qtr}_{dc}")
                    th.min_i = 4 * (half * 2 + qtr) + 4
                    out[u].append(th)

                def ev(e=None, ps=ps, psk=psk, dst_t=dst_t, dst_key=dst_key,
                       half=half, qtr=qtr, tg=tg):
                    p.op("dve", lambda e: e.tensor_copy(
                        dst_t[:, (half * 2 + qtr) * 512:(half * 2 + qtr) * 512 + 512],
                        ps[:, qtr * 512:(qtr + 1) * 512]),
                        reads=[psk], writes=[(dst_key, half * 2 + qtr)],
                        tag=f"p2b{b}f{fc}{tg}e{half}{qtr}")
                ev.min_i = 4 * (half * 2 + qtr) + 4
                out[u].append(ev)

        def v_group(u, half):
            ps = psX[host_ps[u]]
            psk = PSX[host_ps[u]]
            for tcl in range(8):
                tc = half * 8 + tcl
                for dc0 in range(0, NDC, 4):
                    def th(e=None, tcl=tcl, tc=tc, dc0=dc0, ps=ps, psk=psk):
                        for dc in range(dc0, dc0 + 4):
                            p.op("pe", lambda e, tcl=tcl, tc=tc, dc=dc, ps=ps: e.matmul(
                                ps[:, tcl * 128:(tcl + 1) * 128],
                                lnxt[:, dc * T + tc * 128:dc * T + (tc + 1) * 128],
                                slotC[:, dc * D + fc * 128:dc * D + (fc + 1) * 128],
                                start=(dc == 0), stop=(dc == NDC - 1)),
                                reads=[("lnxt", tc), ("slotC", fc)], writes=[psk],
                                tag=f"p2b{b}f{fc}v{tc}_{dc}")
                    th.min_i = tc + 1
                    out[u].append(th)

                def ev(e=None, ps=ps, psk=psk, tcl=tcl, tc=tc):
                    p.op("dve", lambda e: e.tensor_copy(
                        vp[par][:].rearrange("p (tc lh c) -> p tc lh c",
                                             tc=NTC, lh=2)[:, tc, :, 0:64],
                        ps[:, tcl * 128:(tcl + 1) * 128]
                        .rearrange("p (lh s) -> p lh s", lh=2)),
                        reads=[psk], writes=[(f"vp{par}", tc)],
                        tag=f"p2b{b}f{fc}ve{tc}")
                ev.min_i = tc + 1
                out[u].append(ev)

        if alt_qk is None:
            qk_group(0, qf[par], f"qf{par}", slotB, "slotB", 0, "q")
            qk_group(1, qf[par], f"qf{par}", slotB, "slotB", 1, "q")
            qk_group(0, kf[par], f"kf{par}", slotA, "slotA", 0, "k")
            qk_group(1, kf[par], f"kf{par}", slotA, "slotA", 1, "k")
        else:
            mt, bq, bk, kq, kk = alt_qk
            cq = lambda dc: bq + dc * 128
            ck = lambda dc: bk + dc * 128
            qk_group(0, qf[par], f"qf{par}", mt, None, 0, "q", cq, kq)
            qk_group(1, qf[par], f"qf{par}", mt, None, 1, "q", cq, kq)
            qk_group(0, kf[par], f"kf{par}", mt, None, 0, "k", ck, kk)
            qk_group(1, kf[par], f"kf{par}", mt, None, 1, "k", ck, kk)
        v_group(2, 0)
        v_group(3, 1)
        return out

    NU = NFC * NQW

    def emit_s_pair(b, uidx, kc):
        fc, qw = divmod(uidx, NQW)
        par = fc % 2
        q0 = qw * 512
        sp = kc % 2
        p.op("pe", lambda e: e.matmul(
            psS[sp][:, 0:512], kf[par][0:64, kc * 128:(kc + 1) * 128],
            qf[par][0:64, q0:q0 + 512], start=True, stop=True,
            tile_position=(0, 0)),
            reads=[(f"kf{par}", kc // 4), (f"qf{par}", qw)],
            writes=[PSS[sp]], tag=f"u{uidx}sA{kc}")
        p.op("pe", lambda e: e.matmul(
            psS[sp][:, 512:1024], kf[par][64:128, kc * 128:(kc + 1) * 128],
            qf[par][64:128, q0:q0 + 512], start=True, stop=True,
            tile_position=(64, 0)),
            reads=[(f"kf{par}", kc // 4), (f"qf{par}", qw)],
            writes=[PSS[sp]], tag=f"u{uidx}sB{kc}")
        p.op("act", lambda e: e.activation(
            pt[sp][:, :], psS[sp][:, :], AF.Exp, scale=SCALE),
            reads=[PSS[sp]], writes=[f"pt{sp}"], tag=f"u{uidx}x{kc}")

    def emit_av(b, uidx, kc):
        fc, qw = divmod(uidx, NQW)
        par = fc % 2
        uc = uidx % 2
        psA = psX[uc]
        psk = PSX[uc]
        sp = kc % 2
        p.op("pe", lambda e: e.matmul(
            psA[0:65, 0:512], vp[par][:, kc * 130:kc * 130 + 65],
            pt[sp][:, 0:512], start=(kc == 0), stop=(kc == KCN - 1),
            tile_position=(0, 0)),
            reads=[(f"vp{par}", kc), f"pt{sp}"], writes=[psk],
            tag=f"u{uidx}aA{kc}")
        p.op("pe", lambda e: e.matmul(
            psA[0:65, 512:1024], vp[par][:, kc * 130 + 65:kc * 130 + 130],
            pt[sp][:, 512:1024], start=(kc == 0), stop=(kc == KCN - 1),
            tile_position=(0, 0)),
            reads=[(f"vp{par}", kc), f"pt{sp}"], writes=[psk],
            tag=f"u{uidx}aB{kc}")

    def emit_tail(b, uidx):
        fc, qw = divmod(uidx, NQW)
        uc = uidx % 2
        psA = psX[uc]
        psk = PSX[uc]
        q0 = qw * 512
        p.op("act", lambda e: e.activation(fbuf[0][64:65, :], psA[64:65, :], AF.Ln),
             reads=[psk], writes=["fbuf0"], tag=f"u{uidx}rc1")
        p.op("act", lambda e: e.activation(fbuf[0][64:65, :], fbuf[0][64:65, :],
                                           AF.Exp, scale=-1.0),
             reads=["fbuf0"], writes=["fbuf0"], tag=f"u{uidx}rc2")
        p.dma("pool", lambda e: e.dma_start(rrd[uc:uc + 1, :], fbuf[0][64:65, :]),
              reads=["fbuf0"], writes=[f"rrd{uc}"], tag=f"u{uidx}rd")
        p.dma("pool", lambda e: e.dma_start(
            rbsb[:, 0:512],
            bass.AP(rrd_t, uc * 1024, [[0, 64], [1, 512]])),
            reads=[f"rrd{uc}"], writes=["rbsbA"], tag=f"u{uidx}ba")
        p.dma("pool", lambda e: e.dma_start(
            rbsb[:, 512:1024],
            bass.AP(rrd_t, uc * 1024 + 512, [[0, 64], [1, 512]])),
            reads=[f"rrd{uc}"], writes=["rbsbB"], tag=f"u{uidx}bb")
        c0 = fc * T + q0
        p.op("dve", lambda e: e.tensor_mul(
            aofm[0:64, c0:c0 + 512], psA[0:64, 0:512],
            rbsb[:, 0:512]),
            reads=[psk, "rbsbA"], writes=[("aofm", uidx, "A")], tag=f"u{uidx}nA")
        p.op("dve", lambda e: e.tensor_mul(
            stg2[:, uc * 512:(uc + 1) * 512], psA[0:64, 512:1024],
            rbsb[:, 512:1024]),
            reads=[psk, "rbsbB"], writes=[f"stg2{uc}"], tag=f"u{uidx}nB")
        p.dma("pool", lambda e: e.dma_start(
            aofm[64:128, c0:c0 + 512], stg2[:, uc * 512:(uc + 1) * 512]),
            reads=[f"stg2{uc}"], writes=[("aofm", uidx, "B")], tag=f"u{uidx}sh")

    def stats_thunks(bn, tagp):
        ths = []
        for tc in range(NTC):
            def th(tc=tc):
                xti, xkey = xt[tc % 3], f"xt{tc % 3}"
                p.dma("sync", lambda e: e.dma_start(
                    xti[:], x_d[bn, tc * 128:(tc + 1) * 128, :]),
                    writes=[xkey], tag=f"{tagp}_ld{tc}")
                p.op("dve", lambda e: e.tensor_reduce(
                    sxal[:, tc:tc + 1], xti[:], AX.X, ALU.add),
                    reads=[xkey], writes=[("sxa", tc)], tag=f"{tagp}_sx{tc}")
                p.op("dve", lambda e: e.scalar_tensor_tensor(
                    jnk[:], xti[:], 1.0, xti[:], ALU.mult, ALU.mult,
                    accum_out=sqal[:, tc:tc + 1]),
                    reads=[xkey], writes=["jnk", ("sqa", tc)],
                    tag=f"{tagp}_sq{tc}")
            ths.append(th)

        def tail():
            rd = [("sxa", t) for t in range(NTC)] + [("sqa", t) for t in range(NTC)]
            p.op("dve", lambda e: e.tensor_scalar(
                sxal[:, :], sxal[:, :], INVD, None, ALU.mult),
                reads=rd[:NTC], writes=[("sxa", t) for t in range(NTC)],
                tag=f"{tagp}_m")
            p.op("dve", lambda e: e.tensor_scalar(
                sqal[:, :], sqal[:, :], INVD, EPS, ALU.mult, ALU.add),
                reads=rd[NTC:], writes=[("sqa", t) for t in range(NTC)],
                tag=f"{tagp}_q")
            p.op("dve", lambda e: e.tensor_mul(
                rstash[:, :], sxal[:, :], sxal[:, :]),
                reads=rd[:NTC], writes=["rstash"], tag=f"{tagp}_t1s")
            p.op("dve", lambda e: e.tensor_sub(
                rstash[:, :], sqal[:, :], rstash[:, :]),
                reads=rd[NTC:] + ["rstash"], writes=["rstash"], tag=f"{tagp}_v")
            p.op("act", lambda e: e.activation(
                rstash[:, :], rstash[:, :], AF.Ln),
                reads=["rstash"], writes=["rstash"], tag=f"{tagp}_ln")
            p.op("act", lambda e: e.activation(
                rstash[:, :], rstash[:, :], AF.Exp, scale=-0.5),
                reads=["rstash"], writes=["rstash"], tag=f"{tagp}_ex")
            p.op("dve", lambda e: e.tensor_scalar(
                nstash[:, :], sxal[:, :], -1.0, None, ALU.mult),
                reads=rd[:NTC], writes=["nstash"], tag=f"{tagp}_nm")
            p.op("dve", lambda e: e.tensor_mul(
                nstash[:, :], nstash[:, :], rstash[:, :]),
                reads=["nstash", "rstash"], writes=["nstash"], tag=f"{tagp}_nb")
        ths.append(tail)
        return ths

    def ph2(b, boundary_preps=()):
        bpreps = list(boundary_preps)
        for j in range(2):
            p.op("dve", lambda e, j=j: e.memset(
                vp[j][:].rearrange("p (tc lh c) -> p tc lh c", tc=NTC, lh=2)
                [:, :, :, 64:65], 1.0),
                writes=[(f"vp{j}", t) for t in range(NTC)],
                tag=f"vones{b}{j}")
        fillers = {u: [] for u in range(NU)}
        for fc in range(NFC - 1):
            u0 = fc * NQW
            host = [(u0 + u + 1) % 2 for u in range(4)]
            for u, grp in enumerate(proj_thunks(fc + 1, b, host_ps=host)):
                fillers[u0 + u] = grp
        FILL_START = 8
        emit_s_pair(b, 0, 0)
        for g in range(NU * KCN):
            uidx, kc = divmod(g, KCN)
            if g + 1 < NU * KCN:
                u2, kc2 = divmod(g + 1, KCN)
                emit_s_pair(b, u2, kc2)
            rem = fillers[uidx]
            if rem and kc >= FILL_START:
                npop = -(-len(rem) // (KCN - kc))
                for _ in range(min(npop, len(rem))):
                    rem.pop(0)()
            emit_av(b, uidx, kc)
            if kc == KCN - 1:
                for th in rem:
                    th()
                rem.clear()
                emit_tail(b, uidx)
                npp = 0
                nppmax = 8 if uidx >= 26 else 3
                while bpreps and bpreps[0][0] <= uidx and npp < nppmax:
                    bpreps.pop(0)[1]()
                    npp += 1
        for mu, th in bpreps:
            th()

    def ph34(b, extra=()):
        bn = b + 1

        def s1(tc):
            for half in range(2):
                for ec in range(NDC):
                    p.op("pe", lambda e, half=half, ec=ec: e.matmul(
                        psS[0][:, half * 512:(half + 1) * 512],
                        aofm[:, ec * T + tc * 128:ec * T + (tc + 1) * 128],
                        slotA[:, ec * D + half * 512:ec * D + (half + 1) * 512],
                        start=(ec == 0), stop=False),
                        reads=[("aofm", ec * 4 + tc // 4, "A"),
                               ("aofm", ec * 4 + tc // 4, "B")]
                        + [("slotA", half * 4 + k) for k in range(4)],
                        writes=["psS0"], tag=f"p3b{b}t{tc}u{half}_{ec}")
                p.op("pe", lambda e, half=half: e.matmul(
                    psS[0][:, half * 512:(half + 1) * 512], onesbr[:],
                    bub[0:1, half * 512:(half + 1) * 512],
                    start=False, stop=True),
                    reads=["onesbr", "bub"], writes=["psS0"],
                    tag=f"p3b{b}t{tc}ub{half}")

        def s2(tc):
            j = tc % 2
            yti, ykey = yt[j], f"yt{j}"
            p.op("act", lambda e: e.activation(
                yti[:], psS[0][:, :], AF.Identity, accum_out=fam1["sx"][j][:]),
                reads=["psS0"], writes=[ykey, fk(fam1, "sx", j)],
                tag=f"p3b{b}t{tc}ev")
            p.op("act", lambda e: e.activation(
                psS[0][:, :], psS[0][:, :], AF.Square, accum_out=fam1["sq"][j][:]),
                reads=["psS0"], writes=["psS0", fk(fam1, "sq", j)],
                tag=f"p3b{b}t{tc}sq")

        def s3(tc):
            j = tc % 2
            xti, xkey = xt[tc % 3], f"xt{tc % 3}"
            stats_tail(fam1, j, f"p3b{b}t{tc}")
            p.dma("sync", lambda e: e.dma_start(
                xti[:], x_d[b, tc * 128:(tc + 1) * 128, :]),
                writes=[xkey], tag=f"p3b{b}_ld{tc}")

        def s45a(tc):
            j = tc % 2
            xti, xkey = xt[tc % 3], f"xt{tc % 3}"
            yti, ykey = yt[j], f"yt{j}"
            p.op("dve", lambda e: e.tensor_scalar(
                xti[:], xti[:], fam1["nbv"][j][:], None, ALU.add),
                reads=[xkey, fk(fam1, "nbv", j)], writes=[xkey],
                tag=f"p3b{b}t{tc}xnb")
            p.op("dve", lambda e: e.scalar_tensor_tensor(
                xti[:], yti[:], fam1["rstd"][j][:], xti[:], ALU.mult, ALU.add,
                accum_out=fam2["sx"][j][:]),
                reads=[xkey, ykey, fk(fam1, "rstd", j)],
                writes=[xkey, fk(fam2, "sx", j)], tag=f"p3b{b}t{tc}res")
            p.dma("pool", lambda e: e.dma_start(
                xp_d[tc * 128:(tc + 1) * 128, :], xti[:]),
                reads=[xkey], writes=[("xp", tc)], tag=f"p3b{b}_bo{tc}")

        def s45b(tc):
            j = tc % 2
            xti, xkey = xt[tc % 3], f"xt{tc % 3}"
            p.op("act", lambda e: e.activation(
                psO[:, :], xti[:], AF.Square, accum_out=fam2["sq"][j][:]),
                reads=[xkey], writes=["psO", fk(fam2, "sq", j)],
                tag=f"p3b{b}t{tc}sq2")
            stats_tail(fam2, j, f"p3Bb{b}t{tc}", with_nb=False)
            p.op("dve", lambda e: e.tensor_scalar(
                lnb[j][:], xti[:], fam2["t1"][j][:], fam2["rstd"][j][:],
                ALU.subtract, ALU.mult),
                reads=[xkey, fk(fam2, "t1", j), fk(fam2, "rstd", j)],
                writes=[f"lnb{j}"], tag=f"p3b{b}t{tc}_l2")

        def s6(tc):
            j = tc % 2
            transpose_to_lnxt(lnb[j], f"lnb{j}", tc, f"p3b{b}t{tc}")
            for half in range(2):
                for dc in range(NDC):
                    p.op("pe", lambda e, half=half, dc=dc: e.matmul(
                        psS[1][:, half * 512:(half + 1) * 512],
                        lnxt[:, dc * T + tc * 128:dc * T + (tc + 1) * 128],
                        slotB[:, dc * D + half * 512:dc * D + (half + 1) * 512],
                        start=(dc == 0), stop=(dc == NDC - 1)),
                        reads=[("lnxt", tc)] + [("slotB", half * 4 + k) for k in range(4)],
                        writes=["psS1"],
                        tag=f"p4b{b}t{tc}f{half}_{dc}")

        def s78a(tc):
            j = tc % 2
            fti, fky = fbuf[j], f"fbuf{j}"
            x2 = xt[(tc + 1) % 3]
            x2key = f"xt{(tc + 1) % 3}"
            p.op("act", lambda e: e.activation(
                fti[:], psS[1][:, :], AF.Identity, accum_out=fam3["sx"][j][:]),
                reads=["psS1"], writes=[fky, fk(fam3, "sx", j)],
                tag=f"p4b{b}t{tc}ev")
            p.dma("sync", lambda e: e.dma_start(
                x2[:], xp_d[tc * 128:(tc + 1) * 128, :]),
                reads=[("xp", tc)], writes=[x2key], tag=f"p4b{b}_ld{tc}")
            p.op("act", lambda e: e.activation(
                psS[1][:, :], psS[1][:, :], AF.Square, accum_out=fam3["sq"][j][:]),
                reads=["psS1"], writes=["psS1", fk(fam3, "sq", j)],
                tag=f"p4b{b}t{tc}sq")

        def s78b(tc):
            j = tc % 2
            fti, fky = fbuf[j], f"fbuf{j}"
            x2 = xt[(tc + 1) % 3]
            x2key = f"xt{(tc + 1) % 3}"
            stats_tail(fam3, j, f"p4b{b}t{tc}")
            p.op("dve", lambda e: e.tensor_scalar(
                x2[:], x2[:], fam3["nbv"][j][:], None, ALU.add),
                reads=[x2key, fk(fam3, "nbv", j)], writes=[x2key],
                tag=f"p4b{b}t{tc}xnb")
            p.op("dve", lambda e: e.scalar_tensor_tensor(
                x2[:], fti[:], fam3["rstd"][j][:], x2[:], ALU.mult, ALU.add),
                reads=[x2key, fky, fk(fam3, "rstd", j)], writes=[x2key],
                tag=f"p4b{b}t{tc}res")
            p.dma("pool", lambda e: e.dma_start(
                out_d[b, tc * 128:(tc + 1) * 128, :], x2[:]),
                reads=[x2key], tag=f"p4b{b}_st{tc}")

        def r1(tc):
            xti = fbuf[tc % 2]
            fky = f"fbuf{tc % 2}"
            p.dma("sync", lambda e: e.dma_start(
                xti[:], x_d[bn, tc * 128:(tc + 1) * 128, :]),
                writes=[fky], tag=f"r1b{bn}_ld{tc}")
            p.op("act", lambda e: e.activation(
                pt[tc % 2][:, :], xti[:], AF.Identity,
                bias=nstash[:, tc:tc + 1], scale=rstash[:, tc:tc + 1]),
                reads=[fky, "nstash", "rstash"], writes=[f"pt{tc % 2}"],
                tag=f"r1b{bn}_ap{tc}")

        def r2(tc):
            for dc in range(NDC):
                p.op("pe", lambda e, dc=dc: e.matmul(
                    psO[:, dc * 128:(dc + 1) * 128],
                    pt[tc % 2][:, dc * 128:(dc + 1) * 128], idn[:],
                    start=True, stop=True),
                    reads=[f"pt{tc % 2}", "idn"], writes=["psO"],
                    tag=f"r2b{bn}_t{tc}_{dc}")
            p.op("act", lambda e: e.activation(
                lnxt[:].rearrange("p (dc t) -> p dc t", dc=NDC)[:, :, tc * 128:(tc + 1) * 128],
                psO[:, :].rearrange("p (dc f) -> p dc f", dc=NDC), AF.Identity),
                reads=["psO"], writes=[("lnxt", tc)], tag=f"r2b{bn}_ev{tc}")

        extq = list(extra)
        nsteps = NTC + 6
        for i in range(nsteps):
            if 5 <= i < NTC + 5:
                s78a(i - 5)
            if 4 <= i < NTC + 4:
                s6(i - 4)
            if 3 <= i < NTC + 3:
                s45a(i - 3)
            if 1 <= i < NTC + 1:
                s2(i - 1)
            if i < NTC:
                s1(i)
            if 3 <= i < NTC + 3:
                s45b(i - 3)
            if 2 <= i < NTC + 2:
                s3(i - 2)
            if 5 <= i < NTC + 5:
                s78b(i - 5)
            if bn < NB:
                if 4 <= i < NTC + 4:
                    r1(i - 4)
                if 5 <= i < NTC + 5:
                    r2(i - 5)
                npop = 5
                while npop > 0 and extq and getattr(extq[0], "min_i", 0) <= i:
                    extq.pop(0)()
                    npop -= 1
        for th in extq:
            th()

    def disha_thunks(tagp):
        ths = []
        for dc in range(NDC):
            def th(dc=dc):
                p.op("dve", lambda e: e.tensor_add(
                    slotB[:, dc * D:dc * D + D], slotB[:, dc * D:dc * D + D],
                    d2[:]),
                    reads=["d2"] + [("slotB", k) for k in range(NDC)],
                    writes=[("slotB", k) for k in range(NDC)],
                    tag=f"{tagp}_{dc}")
            ths.append(th)
        return ths

    for b in range(NB):
        if b == 0:
            wkt = prep_weight_thunks(w_d["wk"], slotA, "slotA", mku_d, "pk0",
                                     tri="up", evac_eng="act")
            wqt = prep_weight_thunks(w_d["wq"], slotB, "slotB", mkl_d,
                                     "pq0", tri="low", widx=1, evac_eng="act")
            wvt = prep_weight_thunks(w_d["wv"], slotC, "slotC", None, "pv0",
                                     evac_eng="act")
            preps = []
            for fb in range(NDC):
                for th in (wkt[fb], wqt[fb], wvt[fb]):
                    th.min_i = fb
                    preps.append(th)
            preps.extend(t for grp in proj_thunks(0, 0, host_ps=[1, 1, 1, 1])
                         for t in grp)
            preps.sort(key=lambda t: getattr(t, "min_i", 0))
            ph1(0, preps)
        bp = ([(28 + i // 4, th) for i, th in
               enumerate(prep_weight_thunks(w_d["wf"], slotB, "slotB",
                                            None, f"pf{b}"))]
              + [(30, th) for th in disha_thunks(f"pd{b}")]
              + [(26 + i // 4, th) for i, th in
                 enumerate(prep_weight_thunks(w_d["wu"], slotA,
                                              "slotA", None, f"pu{b}"))])
        if b + 1 < NB:
            bp += [(13 + tc, th) for tc, th in
                   enumerate(stats_thunks(b + 1, f"s1b{b + 1}"))]
            bp += [(30, mini_prep_thunk(w_d["wk"], mku_d, 0, "miniK",
                                        f"mk{b + 1}")),
                   (31, mini_prep_thunk(w_d["wq"], mkl_d, D, "miniQ",
                                        f"mq{b + 1}", ndc=1))]
        bp.sort(key=lambda x: x[0])
        ph2(b, bp)
        extra = []
        if b + 1 < NB:
            wkp = prep_weight_thunks(w_d["wk"], slotA, "slotA", mku_d,
                                     f"pk{b + 1}", tri="up")
            wqp = prep_weight_thunks(w_d["wq"], slotB, "slotB", mkl_d,
                                     f"pq{b + 1}", tri="low")
            for th in wkp:
                th.min_i = NTC
            for th in wqp:
                th.min_i = NTC + 4
            grps = proj_thunks(0, b + 1, host_ps=[1, 1, 1, 1],
                               alt_qk=(mini, D, 0, "miniQ", "miniK"))
            for grp in grps:
                for th in grp:
                    th.min_i = getattr(th, "min_i", 0) + 5
            extra = sorted(grps[0] + grps[1] + grps[2] + grps[3],
                           key=lambda t: t.min_i)
            extra += wkp + wqp
        ph34(b, extra)

    with es:
        p.emit(nc)
    return nc


_NC_CACHE = None


def kernel(L0_attn_x, L0_attn_Wk, L0_attn_Wq, L0_attn_Wv, L0_attn_Wu,
           L0_attn_bu, L0_ffn_weight, L0_ffn_disha):
    global _NC_CACHE
    x = np.ascontiguousarray(np.asarray(L0_attn_x, dtype=np.float32))
    wk = np.ascontiguousarray(np.asarray(L0_attn_Wk, dtype=np.float32))
    wq = np.ascontiguousarray(np.asarray(L0_attn_Wq, dtype=np.float32))
    wv = np.ascontiguousarray(np.asarray(L0_attn_Wv, dtype=np.float32))
    wu = np.ascontiguousarray(np.asarray(L0_attn_Wu, dtype=np.float32))
    bu = np.ascontiguousarray(np.asarray(L0_attn_bu, dtype=np.float32)).reshape(1, D)
    wf = np.ascontiguousarray(np.asarray(L0_ffn_weight, dtype=np.float32))
    dh = np.ascontiguousarray(np.asarray(L0_ffn_disha, dtype=np.float32))
    mku = np.triu(np.ones((D, D), dtype=np.float32)).astype(ml_dtypes.bfloat16)
    mkl = np.tril(np.ones((D, D), dtype=np.float32)).astype(ml_dtypes.bfloat16)
    idn = np.eye(128, dtype=np.float32).astype(ml_dtypes.bfloat16)

    if _NC_CACHE is None:
        _NC_CACHE = build_nc()
    nc = _NC_CACHE

    in_maps = []
    for c in range(8):
        in_maps.append({
            "x": x[2 * c:2 * c + 2],
            "wk": wk, "wq": wq, "wv": wv, "wu": wu, "wf": wf,
            "bu": bu, "disha": dh, "mku": mku, "mkl": mkl, "idn": idn,
        })
    res = run_bass_kernel_spmd(nc, in_maps, core_ids=list(range(8)))
    return np.concatenate([np.asarray(r["out"]) for r in res.results], axis=0)
